# version 15
# baseline (speedup 1.0000x reference)
import numpy as np
import concourse.bass as bass
import concourse.mybir as mybir
from concourse.bass_utils import run_bass_kernel_spmd
from contextlib import ExitStack

F32 = mybir.dt.float32
BF16 = mybir.dt.bfloat16
AF = mybir.ActivationFunctionType
ALU = mybir.AluOpType
AX = mybir.AxisListType

D = 2048
NCH = 16
TOK = 1024
HEADS = 16
EPS_LN = 1e-5
EPS_ATTN = 1e-6
EPS_RT = 1e-9
NEXP = 8
HSH = 8192
HRT = 1024

ALL_STAGES = ("ln", "kv", "q", "attn", "router", "shared", "routed")
STAGES = ALL_STAGES
DEBUG_OUT = None


class Bld:
    ENGINES = ("sync", "gpsimd", "tensor", "vector", "scalar")

    def __init__(self):
        self.streams = {e: [] for e in self.ENGINES}
        self.cnt = {s: 0 for s in ("sq", "gq", "pe", "dv", "ac", "cs", "bq",
                                   "w0", "w1", "w2", "w3", "w4", "w5", "w6",
                                   "w7", "m0",
                                   "m1", "m2", "m3", "xo", "ba", "bb", "cc1")}
        self.semh = {}
        self.lastwait = {e: {} for e in self.ENGINES}

    def emit(self, eng, fn):
        self.streams[eng].append(fn)

    def wait(self, eng, sem, val):
        if val is None or val <= 0:
            return
        if self.lastwait[eng].get(sem, 0) >= val:
            return
        self.lastwait[eng][sem] = val

        def f(e, sem=sem, val=val):
            e.wait_ge(self.semh[sem], val)
        self.emit(eng, f)

    def waits(self, eng, ws):
        for w in ws:
            if w is not None:
                self.wait(eng, w[0], w[1])

    def op(self, eng, meth, *args, waits=(), inc=None, **kwargs):
        self.waits(eng, waits)
        val = None
        if inc is not None:
            self.cnt[inc] += 1
            val = self.cnt[inc]

        def f(e, meth=meth, args=args, kwargs=kwargs, inc=inc):
            ins = getattr(e, meth)(*args, **kwargs)
            if inc is not None:
                ins.then_inc(self.semh[inc], 1)
        self.emit(eng, f)
        return val

    def dma(self, queue, out, in_, waits=(), sem=None):
        if sem is None:
            sem = {"sync": "sq", "gpsimd": "gq"}[queue]
        self.waits(queue, waits)
        self.cnt[sem] += 16
        val = self.cnt[sem]

        def f(e, out=out, in_=in_, sem=sem):
            e.dma_start(out=out, in_=in_).then_inc(self.semh[sem], 16)
        self.emit(queue, f)
        return val

    def chain(self, out, pairs, waits=(), inc_last=True):
        n = len(pairs)
        v = None
        for i, (l, r, w) in enumerate(pairs):
            ws = (tuple(waits) if i == 0 else ()) + tuple(w)
            v = self.op("tensor", "matmul", out, l, r, waits=ws,
                        inc=("pe" if (inc_last and i == n - 1) else None),
                        start=(i == 0), stop=(i == n - 1))
        return v


class Slot:
    def __init__(self, ap, sem=None):
        self.ap = ap
        self.sem = sem
        self.ready = None
        self.free = None


def build(stages=None, debug=None):
    stages = tuple(STAGES if stages is None else stages)
    debug = DEBUG_OUT if debug is None else debug
    assert stages == ALL_STAGES[:len(stages)], "stages must be a prefix"
    has = set(stages)

    nc = bass.Bass("TRN2", target_bir_lowering=False)
    b = Bld()
    dr = {}

    def din(name, shape, dt=F32):
        dr[name] = nc.dram_tensor(name, shape, dt, kind="ExternalInput").ap()
        return dr[name]

    xT_mine = din("xT_mine", [D, TOK])
    qwP = din("qwP", [16, 128, 2048]); kwP = din("kwP", [8, 128, 4096])
    vwP = din("vwP", [8, 128, 4096]); owP = din("owP", [16, 128, 2048])
    rwP = din("rwP", [128, 128])
    sw1P = din("sw1P", [64, 128, 2048]); sw2P = din("sw2P", [64, 128, 2048])
    sw3P = din("sw3P", [128, 128, 1024])
    w12P = din("w12P", [NEXP, 16, 128, 2048])
    w3P = din("w3P", [NEXP, 16, 128, 1024])
    qb = din("qb", [1, D]); kb = din("kb", [1, D])
    vb = din("vb", [1, D]); ob = din("ob", [1, D])
    ebias = din("ebias", [1, NEXP])
    eye_in = din("eye_in", [128, 128])
    ones_f32_in = din("ones_f32_in", [1, 128])
    ones_bfr_in = din("ones_bfr_in", [1, 512], BF16)
    ones_bfc_in = din("ones_bfc_in", [128, 1], BF16)
    sel_in = din("sel_in", [8, 8 * 128], BF16)
    outT = nc.dram_tensor("outT", [D, TOK], F32, kind="ExternalOutput").ap()
    cc_in = nc.dram_tensor("cc_in", [128, HEADS * 129], BF16, kind="Internal").ap()
    cc_out = nc.dram_tensor("cc_out", [128, HEADS * 129], BF16, kind="Internal").ap()

    with ExitStack() as ctx:
        def sbuf(name, shape, dt):
            return ctx.enter_context(nc.sbuf_tensor(name, shape, dt))

        acc = sbuf("acc", [128, NCH * 1024], F32)
        xn = sbuf("xn", [128, NCH * 1024], BF16)
        mid = sbuf("mid", [128, 16384], BF16)
        kvs = sbuf("kvs", [128, HEADS * 129], BF16)
        wr = [sbuf(f"wr{i}", [128, 2048], BF16) for i in range(8)]
        sf = [sbuf(f"sf{i}", [128, 512], F32) for i in range(2)]
        sfz = sbuf("sfz", [128, 512], F32)
        zrow = sbuf("zrow", [1, 1024], F32)
        zrec = sbuf("zrec", [1, 1024], BF16)
        sm = sbuf("sm", [128, 2056], F32)
        browA = sbuf("browA", [1, D], BF16)
        browB = sbuf("browB", [1, D], BF16)
        seye = sbuf("seye", [128, 128], F32)
        sones = sbuf("sones", [1, 128], F32)
        sonesbr = sbuf("sonesbr", [1, 512], BF16)
        sonesbc = sbuf("sonesbc", [128, 1], BF16)
        ssel = sbuf("ssel", [8, 8 * 128], BF16)
        sebias = sbuf("sebias", [1, NEXP], F32)
        srw = sbuf("srw", [128, NCH * NEXP], BF16)
        scomb = sbuf("scomb", [8, 1024], BF16)

        pbt = [ctx.enter_context(nc.psum_tensor(f"pb{i}", [128, 512], F32))
               for i in range(8)]
        banks = [Slot(t) for t in pbt]

        for s in ("sq", "gq", "pe", "dv", "ac", "cs", "bq",
                  "w0", "w1", "w2", "w3", "w4", "w5", "w6", "w7",
                  "m0", "m1", "m2", "m3",
                  "xo", "ba", "bb", "cc1"):
            b.semh[s] = ctx.enter_context(nc.semaphore(s))

        SM_EB = 0
        SM_AROW = 8
        SM_ZR = 8
        SM_SC = 520
        SM_CT = 584
        SM_W = 648
        SM_BROW = 1032
        SM_M2 = 1544

        def accs(d, lo=0, hi=1024):
            return acc[:, 1024 * d + lo: 1024 * d + hi]

        def xns(d, lo, hi):
            return xn[:, 1024 * d + lo: 1024 * d + hi]

        def mids(i, lo=0, hi=1024):
            return mid[:, 1024 * i + lo: 1024 * i + hi]

        acc_state = {}

        def acc_add(oc, h, bank_ap, vpe):
            ws = [("pe", vpe)]
            if (oc, h) in acc_state:
                ws.append(("dv", acc_state[(oc, h)]))
            v = b.op("vector", "tensor_tensor", accs(oc, 512 * h, 512 * (h + 1)),
                     accs(oc, 512 * h, 512 * (h + 1)), bank_ap, ALU.add,
                     waits=ws, inc="dv")
            acc_state[(oc, h)] = v
            return v

        rt = {"ebb": None, "comb": None}

        def emit_router_init():
            eb = banks[7]
            ws = [("cs", CS_ALL)]
            if eb.free:
                ws.append(eb.free)
            vpe = b.chain(eb.ap[:, 0:8], [(sones[:, :], sebias[0:1, :], [])],
                          waits=ws)
            rt["ebb"] = b.op("vector", "tensor_copy", sm[:, SM_EB:SM_EB + 8],
                             eb.ap[:, 0:8], waits=[("pe", vpe)], inc="dv")
            eb.free = ("dv", rt["ebb"])

        def emit_router_chunk(j, xnv=None):
            W = SM_W
            rb = banks[6 + j % 2]
            ws = [("bq", v_rw), ("dv", v_xn_mine if xnv is None else xnv)]
            if rb.free:
                ws.append(rb.free)
            prs = [(xns(d, 128 * j, 128 * (j + 1)),
                    srw[:, NEXP * d:NEXP * (d + 1)], [])
                   for d in range(NCH)]
            vpe = b.chain(rb.ap[:, 0:8], prs, waits=ws)
            va = b.op("scalar", "activation",
                      sm[:, SM_SC + 8 * j:SM_SC + 8 * (j + 1)],
                      rb.ap[:, 0:8], AF.Sigmoid,
                      waits=[("pe", vpe)], inc="ac")
            rb.free = ("ac", va)
            sc = sm[:, SM_SC + 8 * j:SM_SC + 8 * (j + 1)]
            sel_t = sm[:, W:W + 8]
            m1 = sm[:, W + 32:W + 33]
            m2 = sm[:, W + 33:W + 34]
            ssum = sm[:, W + 34:W + 35]
            rec = sm[:, W + 35:W + 36]
            mask1 = sm[:, W + 8:W + 16]
            mask2 = sm[:, W + 16:W + 24]
            ms = sm[:, W + 24:W + 32]
            v1 = b.op("vector", "tensor_tensor", sel_t, sc,
                      sm[:, SM_EB:SM_EB + 8], ALU.add,
                      waits=[("ac", va), ("dv", rt["ebb"])], inc="dv")
            v2 = b.op("vector", "tensor_reduce", m1, sel_t, AX.X, ALU.max,
                      waits=[("dv", v1)], inc="dv")
            v3 = b.op("vector", "tensor_scalar", mask1, sel_t, m1, None,
                      ALU.is_ge, waits=[("dv", v2)], inc="dv")
            v4 = b.op("vector", "scalar_tensor_tensor", sm[:, W:W + 8],
                      mask1, -1e9, sel_t, ALU.mult, ALU.add,
                      waits=[("dv", v3)], inc="dv")
            v5 = b.op("vector", "tensor_reduce", m2, sm[:, W:W + 8], AX.X,
                      ALU.max, waits=[("dv", v4)], inc="dv")
            v6 = b.op("vector", "tensor_scalar", mask2, sm[:, W:W + 8], m2,
                      None, ALU.is_ge, waits=[("dv", v5)], inc="dv")
            v7 = b.op("vector", "tensor_tensor", mask1, mask1, mask2,
                      ALU.add, waits=[("dv", v6)], inc="dv")
            v8 = b.op("vector", "tensor_tensor", ms, mask1, sc, ALU.mult,
                      waits=[("dv", v7)], inc="dv")
            v9 = b.op("vector", "tensor_reduce", ssum, ms, AX.X, ALU.add,
                      waits=[("dv", v8)], inc="dv")
            v10 = b.op("vector", "tensor_scalar", ssum, ssum,
                       float(EPS_RT), None, ALU.add,
                       waits=[("dv", v9)], inc="dv")
            v11 = b.op("vector", "reciprocal", rec, ssum,
                       waits=[("dv", v10)], inc="dv")
            v12 = b.op("vector", "tensor_scalar",
                       sm[:, SM_CT + 8 * j:SM_CT + 8 * (j + 1)], ms, rec,
                       None, ALU.mult, waits=[("dv", v11)], inc="dv")
            rt[("ct", j)] = v12

        def emit_router_xpose(j):
            tb = banks[6 + (j + 1) % 2]
            ws = [("dv", rt[("ct", j)]), ("cs", CS_ALL)]
            if tb.free:
                ws.append(tb.free)
            vtp = b.op("tensor", "transpose", tb.ap[0:8, 0:128],
                       sm[:, SM_CT + 8 * j:SM_CT + 8 * (j + 1)],
                       seye[:, :], waits=ws, inc="pe")
            vcp = b.op("vector", "tensor_copy",
                       scomb[0:8, 128 * j:128 * (j + 1)], tb.ap[0:8, 0:128],
                       waits=[("pe", vtp)], inc="dv")
            tb.free = ("dv", vcp)
            rt["comb"] = vcp

        b.dma("sync", seye[:, :], eye_in, sem="cs")
        b.dma("sync", sonesbc[:, :], ones_bfc_in, sem="cs")
        b.dma("sync", sones[:, :], ones_f32_in, sem="cs")
        b.dma("sync", sonesbr[:, :], ones_bfr_in, sem="cs")
        b.dma("sync", sebias[:, :], ebias, sem="cs")
        CS_ALL = b.dma("sync", ssel[:, :], sel_in, sem="cs")
        v_rw = b.dma("gpsimd", srw[:, :], rwP, sem="bq")

        sf_free = [None, None]

        def ln_half():
            nonlocal sf_free
            sum_bank = [banks[4], banks[5]]
            sq_bank = [banks[6], banks[7]]
            xb_free = [None, None]
            xq_free = [None, None]
            vmark = {}
            xld = {}
            for d in range(4):
                xld[d] = b.dma("sync" if d % 2 == 0 else "gpsimd", accs(d),
                               xT_mine[128 * d:128 * (d + 1), :],
                               sem=f"m{d % 4}")
            for d in range(NCH):
                src = accs(d)
                xb_t = mids(d % 2)
                xq_t = mids(2 + d % 2)
                va = b.op("scalar", "activation", xb_t, src, AF.Identity,
                          waits=[(f"m{d % 4}", xld[d]), xb_free[d % 2]],
                          inc="ac")
                xb_rdy = ("ac", va)
                va2 = b.op("scalar", "activation", xq_t, src, AF.Square,
                           waits=[xb_rdy, xq_free[d % 2]], inc="ac")
                if d + 4 < NCH:
                    xld[d + 4] = b.dma("sync" if d % 2 == 0 else "gpsimd",
                                       accs(d + 4),
                                       xT_mine[128 * (d + 4):128 * (d + 5), :],
                                       sem=f"m{d % 4}", waits=[("ac", va2)])
                for i, (bankset, tile, w) in enumerate(
                        ((sum_bank, xb_t, xb_rdy),
                         (sq_bank, xq_t, ("ac", va2)))):
                    for h in range(2):
                        bank = bankset[h]
                        ws = [w, ("cs", CS_ALL)]
                        if d == 0 and bank.free:
                            ws.append(bank.free)
                        last = (i == 1 and h == 1)
                        v = b.op("tensor", "matmul", bank.ap[0:1, :],
                                 sonesbc[:, :], tile[:, 512 * h:512 * (h + 1)],
                                 waits=ws,
                                 inc=("pe" if last else None),
                                 start=(d == 0), stop=(d == NCH - 1))
                vmark[d] = v
                xb_free[d % 2] = ("pe", v)
                xq_free[d % 2] = ("pe", v)
            va_rows = []
            for h in range(2):
                srow = sum_bank[h].ap[0:1, :]
                qrow = sq_bank[h].ap[0:1, :]
                wv = [("pe", vmark[NCH - 1])]
                v1 = b.op("vector", "tensor_scalar", srow, srow, 1.0 / D, None,
                          ALU.mult, waits=wv, inc="dv")
                v2 = b.op("scalar", "activation", sm[0:1, SM_M2:SM_M2 + 512],
                          srow, AF.Square, waits=[("dv", v1)], inc="ac")
                v3 = b.op("vector", "scalar_tensor_tensor", qrow, qrow, 1.0 / D,
                          sm[0:1, SM_M2:SM_M2 + 512], ALU.mult, ALU.subtract,
                          waits=[("ac", v2)], inc="dv")
                v3b = b.op("vector", "tensor_scalar", qrow, qrow,
                           float(EPS_LN), None, ALU.add,
                           waits=[("dv", v3)], inc="dv")
                v4 = b.op("scalar", "activation", qrow, qrow, AF.Sqrt,
                          waits=[("dv", v3b)], inc="ac")
                v5 = b.op("vector", "reciprocal",
                          sm[0:1, SM_AROW + 512 * h: SM_AROW + 512 * (h + 1)],
                          qrow, waits=[("ac", v4)], inc="dv")
                v6 = b.op("vector", "tensor_tensor",
                          sm[0:1, SM_BROW + 512 * h: SM_BROW + 512 * (h + 1)],
                          srow,
                          sm[0:1, SM_AROW + 512 * h: SM_AROW + 512 * (h + 1)],
                          ALU.mult, waits=[("dv", v5)], inc="dv")
                va_rows.append(v6)
                sum_bank[h].free = ("dv", v6)
                sq_bank[h].free = ("dv", v6)
            ab_pe = {}
            for h in range(2):
                for k, base in enumerate((SM_AROW, SM_BROW)):
                    bank = banks[2 * k + h]
                    ws = [("dv", va_rows[h]), ("cs", CS_ALL)]
                    if bank.free:
                        ws.append(bank.free)
                    ab_pe[(k, h)] = b.chain(
                        bank.ap[:, :],
                        [(sones[:, :],
                          sm[0:1, base + 512 * h: base + 512 * (h + 1)], [])],
                        waits=ws)
            lastv = None
            halfv = [None, None]
            sf0_war = sf_free[0]
            for h in range(2):
                for d in range(NCH):
                    src = accs(d, 512 * h, 512 * (h + 1))
                    ws = [("pe", ab_pe[(0, h)]), ("pe", ab_pe[(1, h)]), sf0_war]
                    v1 = b.op("vector", "tensor_tensor",
                              sf[0][:, 0:512],
                              src, banks[h].ap[:, :], ALU.mult,
                              waits=ws, inc="dv")
                    v2 = b.op("vector", "tensor_tensor",
                              xns(d, 512 * h, 512 * (h + 1)),
                              sf[0][:, 0:512],
                              banks[2 + h].ap[:, :], ALU.subtract,
                              waits=[("dv", v1)], inc="dv")
                    sf0_war = ("dv", v2)
                    lastv = v2
                halfv[h] = v2
            for i in range(4):
                banks[i].free = ("dv", lastv)
            return halfv

        v_xn_mine = v_xn_h0 = None
        if "ln" in has:
            v_xn_h0, v_xn_mine = ln_half()
            sf_free = [("dv", v_xn_mine), sf_free[1]]
        v_pe_ln = b.cnt["pe"]

        v_kvs_glob = None
        v_kv_pe = None
        wr_free_kv = [None] * 4
        wr_free_q = [None] * 4
        if "kv" in has:
            v_kb = b.dma("gpsimd", browA[0:1, :], kb, sem="ba")
            v_vb = b.dma("gpsimd", browB[0:1, :], vb, sem="bb")
            kgrp = mid[:, 0:2048]
            vgrp = mid[:, 2048:4096]
            wpair = [
                [Slot(wr[0], "w0"), Slot(wr[1], "w1"),
                 Slot(wr[2], "w2"), Slot(wr[3], "w3")],
                [Slot(mid[:, 8192 + 2048 * i:8192 + 2048 * (i + 1)], f"m{i}")
                 for i in range(4)],
            ]
            for sl in wpair[1]:
                sl.free = ("ac", b.cnt["ac"])
            prev_kv_pe = None
            v_kvs_done = None
            hg0_last = None
            qpre = []
            if "q" in has:
                for i in range(2):
                    s = Slot(mid, "xo")
                    s.off = 4096 + 2048 * i
                    qpre.append(s)
            if "router" in has:
                emit_router_init()
                emit_router_chunk(0, v_xn_h0)
                emit_router_chunk(1, v_xn_h0)
                emit_router_xpose(0)
                emit_router_chunk(2, v_xn_h0)
                emit_router_xpose(1)
                emit_router_chunk(3, v_xn_h0)
                emit_router_xpose(2)
            for hg in range(8):
                pair = wpair[hg % 2]
                kt = pair[0:2]
                vt = pair[2:4]
                for tiles, w_dram, queue in ((kt, kwP, "gpsimd"),
                                             (vt, vwP, "gpsimd")):
                    for half in range(2):
                        sl = tiles[half]
                        ws = [sl.free] if sl.free else []
                        if hg == 1:
                            ws.append(hg0_last)
                        v = b.dma(queue, sl.ap[:, :],
                                  w_dram[hg, :, 2048 * half:2048 * (half + 1)],
                                  waits=ws, sem=sl.sem)
                        sl.ready = (sl.sem, v)
                if hg == 0:
                    hg0_last = ("w3", b.cnt["w3"])
                if hg == 5 and "q" in has:
                    for i, sl in enumerate(qpre):
                        v = b.dma("gpsimd",
                                  sl.ap[:, sl.off:sl.off + 2048],
                                  qwP[i, :, :], sem=sl.sem)
                        sl.ready = (sl.sem, v)
                grp_last = None
                for which, (tiles, grp, brow, bsem, vbias) in enumerate(
                        ((kt, kgrp, browA, "ba", v_kb),
                         (vt, vgrp, browB, "bb", v_vb))):
                    for pk in range(4):
                        bank = banks[2 * which + pk % 2]
                        vpe = None
                        for sub in range(2):
                            j = 2 * pk + sub
                            prs = []
                            for d in range(NCH):
                                sl = tiles[d // 8]
                                prs.append((xns(d, 128 * j, 128 * (j + 1)),
                                            sl.ap[:, 256 * (d % 8):
                                                  256 * (d % 8) + 256],
                                            [sl.ready]))
                            prs.append((sonesbr[0:1, 0:128],
                                        brow[0:1, 256 * hg:256 * (hg + 1)],
                                        [(bsem, vbias), ("cs", CS_ALL)]))
                            ws = [("dv", v_xn_h0 if j < 4 else v_xn_mine)]
                            if sub == 0 and bank.free:
                                ws.append(bank.free)
                            vpe = b.chain(
                                bank.ap[:, 256 * sub:256 * (sub + 1)], prs,
                                waits=ws, inc_last=(sub == 1))
                        war = []
                        if prev_kv_pe and pk == 0:
                            war = [("pe", prev_kv_pe)]
                        elif prev_kv_pe is None and pk == 0 and which == 0:
                            war = [("pe", v_pe_ln)]
                        if which == 0:
                            va1 = b.op("scalar", "activation", sf[0][:, :],
                                       bank.ap[:, :], AF.Relu,
                                       waits=[("pe", vpe), sf_free[0]], inc="ac")
                            va2 = b.op("scalar", "activation", sf[1][:, :],
                                       bank.ap[:, :], AF.Relu, scale=-1.0,
                                       waits=[sf_free[1]], inc="ac")
                            va3 = b.op("scalar", "activation", sf[1][:, :],
                                       sf[1][:, :], AF.Exp, scale=-1.0,
                                       waits=[("ac", va2)], inc="ac")
                            vd = b.op("vector", "tensor_tensor",
                                      grp[:, 512 * pk:512 * (pk + 1)],
                                      sf[1][:, :], sf[0][:, :], ALU.add,
                                      waits=[("ac", va1), ("ac", va3)] + war,
                                      inc="dv")
                            sf_free = [("dv", vd), ("dv", vd)]
                        else:
                            vd = b.op("vector", "tensor_copy",
                                      grp[:, 512 * pk:512 * (pk + 1)],
                                      bank.ap[:, :],
                                      waits=[("pe", vpe)] + war, inc="dv")
                        bank.free = ("dv", vd)
                        grp_last = vd
                for t in range(2):
                    hh = 2 * hg + t
                    kvb = banks[4 + hh % 2]
                    ws = [("dv", grp_last)]
                    if kvb.free:
                        ws.append(kvb.free)
                    prs = [(kgrp[:, 256 * j + 128 * t:256 * j + 128 * (t + 1)],
                            vgrp[:, 256 * j + 128 * t:256 * j + 128 * (t + 1)],
                            []) for j in range(8)]
                    b.chain(kvb.ap[:, 0:128], prs, waits=ws, inc_last=False)
                    prs2 = [(kgrp[:, 256 * j + 128 * t:256 * j + 128 * (t + 1)],
                             sonesbc[:, :], []) for j in range(8)]
                    vpe2 = b.chain(kvb.ap[:, 128:129], prs2)
                    prev_kv_pe = vpe2
                    vd = b.op("vector", "tensor_copy",
                              kvs[:, 129 * hh:129 * (hh + 1)], kvb.ap[:, 0:129],
                              waits=[("pe", vpe2)], inc="dv")
                    kvb.free = ("dv", vd)
                    v_kvs_done = vd
                for sl in pair:
                    sl.free = ("pe", prev_kv_pe)
                if "router" in has and hg <= 4:
                    if hg <= 3:
                        emit_router_chunk(4 + hg)
                    emit_router_xpose(3 + hg)
            v_kv_pe = prev_kv_pe
            b.dma("sync", cc_in, kvs[:, :], sem="cc1",
                  waits=[("dv", v_kvs_done)])
            wr_free_kv = [sl.free for sl in wpair[0]]

        def emit_collective():
            b.op("gpsimd", "collective_compute",
                 "AllReduce", ALU.add,
                 [[0, 1], [2, 3], [4, 5], [6, 7]],
                 [cc_in.opt()], [cc_out.opt()],
                 waits=[("cc1", 16)], inc="cc1")
            b.dma("sync", kvs[:, :], cc_out, sem="cc1",
                  waits=[("cc1", 17)])
            return ("cc1", 33)

        if "kv" in has and "q" not in has:
            v_kvs_glob = emit_collective()

        v_qp = {}
        v_q_pe = None
        attn_done = {}
        v_attn_pe = None
        do_attn = "attn" in has
        zrow_free = {0: None, 1: None}
        zrec_free = {0: None, 1: None}
        sfz_free = [None]

        vrec_s = {}
        vat_s = {}

        def attn_stageA(i):
            for h in range(2):
                zb = banks[4 + h]
                ws = [v_kvs_glob, ("dv", v_qp[(i, h)])]
                if zb.free:
                    ws.append(zb.free)
                vz = b.chain(zb.ap[0:1, :],
                             [(kvs[:, 129 * i + 128:129 * (i + 1)],
                               mids(i, 512 * h, 512 * (h + 1)), [])],
                             waits=ws)
                ws2 = [("pe", vz), zrow_free[h]]
                v1 = b.op("vector", "tensor_copy",
                          zrow[0:1, 512 * h:512 * (h + 1)], zb.ap[0:1, :],
                          waits=ws2, inc="dv")
                zb.free = ("dv", v1)
                v2 = b.op("vector", "tensor_scalar",
                          zrow[0:1, 512 * h:512 * (h + 1)],
                          zrow[0:1, 512 * h:512 * (h + 1)],
                          float(EPS_ATTN), None, ALU.add,
                          waits=[("dv", v1)], inc="dv")
                v3 = b.op("vector", "reciprocal",
                          zrow[0:1, 512 * h:512 * (h + 1)],
                          zrow[0:1, 512 * h:512 * (h + 1)],
                          waits=[("dv", v2)], inc="dv")
                v3b = b.op("vector", "tensor_copy",
                           zrec[0:1, 512 * h:512 * (h + 1)],
                           zrow[0:1, 512 * h:512 * (h + 1)],
                           waits=[("dv", v3), zrec_free[h]], inc="dv")
                zrow_free[h] = ("dv", v3b)
                vrec_s[(i, h)] = v3b
            for h in range(2):
                ab = banks[(2 * i + h) % 4]
                ws = [("dv", v_qp[(i, h)])]
                if ab.free:
                    ws.append(ab.free)
                vat_s[(i, h)] = b.chain(ab.ap[:, :],
                                        [(kvs[:, 129 * i:129 * i + 128],
                                          mids(i, 512 * h, 512 * (h + 1)), [])],
                                        waits=ws)

        def attn_stageB(i):
            vbc = {}
            for h in range(2):
                bc = banks[6 + h]
                ws = [("dv", vrec_s[(i, h)]), ("cs", CS_ALL)]
                if bc.free:
                    ws.append(bc.free)
                vbc[h] = b.chain(bc.ap[:, :],
                                 [(sonesbr[0:1, 0:128],
                                   zrec[0:1, 512 * h:512 * (h + 1)], [])],
                                 waits=ws)
                zrec_free[h] = ("pe", vbc[h])
            for h in range(2):
                bc = banks[6 + h]
                ab = banks[(2 * i + h) % 4]
                v4 = b.op("vector", "tensor_copy", sfz[:, :], bc.ap[:, :],
                          waits=[("pe", vbc[h]), sfz_free[0]], inc="dv")
                bc.free = ("dv", v4)
                vd = b.op("vector", "tensor_tensor",
                          mids(i, 512 * h, 512 * (h + 1)),
                          sfz[:, :], ab.ap[:, :], ALU.mult,
                          waits=[("pe", vat_s[(i, h)]), ("dv", v4)], inc="dv")
                sfz_free[0] = ("dv", vd)
                ab.free = ("dv", vd)
                attn_done[(i, h)] = vd

        if "q" in has:
            ring = [Slot(wr[i], f"w{i}") for i in range(4)]
            for i, s in enumerate(ring):
                s.free = wr_free_kv[i] if "kv" in has else None

            def q_load(oc):
                slot = ring[oc % 4]
                ws = [slot.free] if slot.free else []
                v = b.dma("gpsimd", slot.ap[:, :], qwP[oc, :, :], waits=ws,
                          sem=slot.sem)
                slot.ready = (slot.sem, v)

            for oc in range(2, 4):
                q_load(oc)
            v_qb = b.dma("gpsimd", browA[0:1, :], qb, sem="ba",
                         waits=[("pe", v_kv_pe)] if v_kv_pe else [])
            if "kv" in has:
                v_kvs_glob = emit_collective()
            for oc in range(NCH):
                if do_attn and oc >= 6:
                    attn_stageB(oc - 6)
                if do_attn and oc >= 5:
                    attn_stageA(oc - 5)
                slot = qpre[oc] if (oc < 2 and qpre) else ring[oc % 4]
                off = getattr(slot, "off", 0)
                for h in range(2):
                    bank = banks[(2 * oc + h) % 4]
                    prs = [(slot.ap[:, off + 128 * d:off + 128 * (d + 1)],
                            xns(d, 512 * h, 512 * (h + 1)), [])
                           for d in range(NCH)]
                    prs.append((browA[0:1, 128 * oc:128 * (oc + 1)],
                                sonesbr[0:1, :], [("ba", v_qb), ("cs", CS_ALL)]))
                    ws = [slot.ready, ("dv", v_xn_mine)]
                    if bank.free:
                        ws.append(bank.free)
                    vpe = b.chain(bank.ap[:, :], prs, waits=ws)
                    war = ([("pe", v_kv_pe)]
                           if (v_kv_pe and oc < 4) else [])
                    va1 = b.op("scalar", "activation", sf[0][:, :],
                               bank.ap[:, :], AF.Relu,
                               waits=[("pe", vpe), sf_free[0]], inc="ac")
                    va2 = b.op("scalar", "activation", sf[1][:, :],
                               bank.ap[:, :], AF.Relu, scale=-1.0,
                               waits=[sf_free[1]], inc="ac")
                    va3 = b.op("scalar", "activation", sf[1][:, :],
                               sf[1][:, :], AF.Exp, scale=-1.0,
                               waits=[("ac", va2)], inc="ac")
                    vd = b.op("vector", "tensor_tensor",
                              mids(oc, 512 * h, 512 * (h + 1)),
                              sf[1][:, :], sf[0][:, :], ALU.add,
                              waits=[("ac", va1), ("ac", va3)] + war, inc="dv")
                    sf_free = [("dv", vd), ("dv", vd)]
                    bank.free = ("dv", vd)
                    v_qp[(oc, h)] = vd
                slot.free = ("pe", b.cnt["pe"])
                if oc + 4 < NCH:
                    q_load(oc + 4)
            if do_attn:
                attn_stageB(10)
                for i in range(11, HEADS):
                    attn_stageA(i)
                    attn_stageB(i)
            v_q_pe = b.cnt["pe"]
            wr_free_q = [ring[i].free for i in range(4)]

        v_comb = rt["comb"]

        if "attn" in has:
            v_ob = b.dma("gpsimd", browB[0:1, :], ob, sem="bb",
                         waits=[("pe", v_kv_pe)] if v_kv_pe else [])
            ring = [Slot(wr[i], f"w{i}") for i in range(4)]
            for i, s in enumerate(ring):
                s.free = wr_free_q[i] if "q" in has else None

            def o_load(oc):
                slot = ring[oc % 4]
                ws = [slot.free] if slot.free else []
                v = b.dma("gpsimd", slot.ap[:, :], owP[oc, :, :], waits=ws,
                          sem=slot.sem)
                slot.ready = (slot.sem, v)

            for oc in range(4):
                o_load(oc)
            v_attn_all = attn_done[(HEADS - 1, 1)]
            for oc in range(NCH):
                slot = ring[oc % 4]
                for h in range(2):
                    bank = banks[(2 * oc + h) % 4]
                    prs = [(slot.ap[:, 128 * hh2:128 * (hh2 + 1)],
                            mids(hh2, 512 * h, 512 * (h + 1)), [])
                           for hh2 in range(HEADS)]
                    prs.append((browB[0:1, 128 * oc:128 * (oc + 1)],
                                sonesbr[0:1, :], [("bb", v_ob), ("cs", CS_ALL)]))
                    ws = [slot.ready, ("dv", v_attn_all)]
                    if bank.free:
                        ws.append(bank.free)
                    vpe = b.chain(bank.ap[:, :], prs, waits=ws)
                    vd = acc_add(oc, h, bank.ap[:, :], vpe)
                    bank.free = ("dv", vd)
                slot.free = ("pe", b.cnt["pe"])
                if oc + 4 < NCH:
                    o_load(oc + 4)
            v_attn_pe = b.cnt["pe"]
            wr_free_o = [ring[i].free for i in range(4)]

        uring = [Slot(wr[i], f"w{i}") for i in range(6)]
        for i, sl in enumerate(uring):
            sl.free = wr_free_o[i] if (i < 4 and "attn" in has) else None
        dsl = []
        for i, (t, off) in enumerate(((wr[6], 0), (wr[6], 1024),
                                      (wr[7], 0), (wr[7], 1024))):
            s = Slot(t, f"w{6 + i // 2}")
            s.off = off
            dsl.append(s)

        if "shared" in has:
            mid_war = {0: ("pe", v_attn_pe) if v_attn_pe else None,
                       1: ("pe", v_attn_pe) if v_attn_pe else None}

            def up_load4(u):
                p = u % 3
                for slot, wd in ((uring[2 * p], sw1P), (uring[2 * p + 1], sw2P)):
                    ws = [slot.free] if slot.free else []
                    v = b.dma("gpsimd", slot.ap[:, :], wd[u, :, :],
                              waits=ws, sem=slot.sem)
                    slot.ready = (slot.sem, v)

            def dn_load4(u):
                slot = dsl[u % 4]
                ws = [slot.free] if slot.free else []
                v = b.dma("gpsimd", slot.ap[:, slot.off:slot.off + 1024],
                          sw3P[u, :, :], waits=ws, sem=slot.sem)
                slot.ready = (slot.sem, v)

            for u in range(3):
                up_load4(u)
            for u in range(4):
                dn_load4(u)
            for g in range(8):
                buf = g % 2
                p_last = None
                for hc in range(8):
                    u = 8 * g + hc
                    p = u % 3
                    sA, sB = uring[2 * p], uring[2 * p + 1]
                    for h in range(2):
                        bg = banks[h]
                        bu = banks[2 + h]
                        prsg = [(sA.ap[:, 128 * d:128 * (d + 1)],
                                 xns(d, 512 * h, 512 * (h + 1)), [sA.ready])
                                for d in range(NCH)]
                        prsu = [(sB.ap[:, 128 * d:128 * (d + 1)],
                                 xns(d, 512 * h, 512 * (h + 1)), [sB.ready])
                                for d in range(NCH)]
                        wsg = [("dv", v_xn_mine)]
                        wsu = []
                        if bg.free:
                            wsg.append(bg.free)
                        if bu.free:
                            wsu.append(bu.free)
                        vg = b.chain(bg.ap[:, :], prsg, waits=wsg)
                        vu = b.chain(bu.ap[:, :], prsu, waits=wsu)
                        va = b.op("scalar", "activation",
                                  sf[0][:, :], bg.ap[:, :], AF.Silu,
                                  waits=[("pe", vg), sf_free[0]], inc="ac")
                        war = [mid_war[buf]] if hc == 0 and h == 0 else []
                        vd = b.op("vector", "tensor_tensor",
                                  mids(8 * buf + hc, 512 * h, 512 * (h + 1)),
                                  sf[0][:, :], bu.ap[:, :], ALU.mult,
                                  waits=[("ac", va), ("pe", vu)] + war,
                                  inc="dv")
                        sf_free[0] = ("dv", vd)
                        bg.free = ("dv", vd)
                        bu.free = ("dv", vd)
                        p_last = vd
                    sA.free = ("pe", b.cnt["pe"])
                    sB.free = ("pe", b.cnt["pe"])
                    if u + 3 < 64:
                        up_load4(u + 3)
                for oc in range(NCH):
                    u = 16 * g + oc
                    slot = dsl[u % 4]
                    for h in range(2):
                        bank = banks[4 + (2 * oc + h) % 4]
                        prs = [(slot.ap[:, slot.off + 128 * k:
                                        slot.off + 128 * (k + 1)],
                                mids(8 * buf + k, 512 * h, 512 * (h + 1)),
                                []) for k in range(8)]
                        ws = [slot.ready, ("dv", p_last)]
                        if bank.free:
                            ws.append(bank.free)
                        vpe = b.chain(bank.ap[:, :], prs, waits=ws)
                        vd = acc_add(oc, h, bank.ap[:, :], vpe)
                        bank.free = ("dv", vd)
                    slot.free = ("pe", b.cnt["pe"])
                    if u + 4 < 128:
                        dn_load4(u + 4)
                mid_war[buf] = ("pe", b.cnt["pe"])
            mid_war_s = dict(mid_war)

        if "routed" in has:
            h_war = (dict(mid_war_s) if "shared" in has
                     else {0: ("pe", b.cnt["pe"]), 1: ("pe", b.cnt["pe"])})

            def up_load5(u):
                e, j = divmod(u, 8)
                p = u % 3
                for slot, jj in ((uring[2 * p], j), (uring[2 * p + 1], 8 + j)):
                    ws = [slot.free] if slot.free else []
                    v = b.dma("gpsimd", slot.ap[:, :], w12P[e, jj, :, :],
                              waits=ws, sem=slot.sem)
                    slot.ready = (slot.sem, v)

            def dn_load5(u):
                e, oc = divmod(u, 16)
                slot = dsl[u % 4]
                ws = [slot.free] if slot.free else []
                v = b.dma("gpsimd", slot.ap[:, slot.off:slot.off + 1024],
                          w3P[e, oc, :, :], waits=ws, sem=slot.sem)
                slot.ready = (slot.sem, v)

            for u in range(3):
                up_load5(u)
            for u in range(4):
                dn_load5(u)
            for e in range(NEXP):
                buf = e % 2
                cbb = [banks[6], banks[7]]
                cb_pe = {}
                for h in range(2):
                    ws = [("dv", v_comb), ("cs", CS_ALL)]
                    if cbb[h].free:
                        ws.append(cbb[h].free)
                    cb_pe[h] = b.chain(
                        cbb[h].ap[:, :],
                        [(ssel[:, 128 * e:128 * (e + 1)],
                          scomb[0:8, 512 * h:512 * (h + 1)], [])], waits=ws)
                h_ready = None
                for j in range(8):
                    u = 8 * e + j
                    p = u % 3
                    sA, sB = uring[2 * p], uring[2 * p + 1]
                    for h in range(2):
                        b1 = banks[h]
                        b2 = banks[2 + h]
                        prs1 = [(sA.ap[:, 128 * d:128 * (d + 1)],
                                 xns(d, 512 * h, 512 * (h + 1)), [sA.ready])
                                for d in range(NCH)]
                        prs2 = [(sB.ap[:, 128 * d:128 * (d + 1)],
                                 xns(d, 512 * h, 512 * (h + 1)), [sB.ready])
                                for d in range(NCH)]
                        ws1 = []
                        ws2 = []
                        if b1.free:
                            ws1.append(b1.free)
                        if b2.free:
                            ws2.append(b2.free)
                        v1 = b.chain(b1.ap[:, :], prs1, waits=ws1)
                        v2 = b.chain(b2.ap[:, :], prs2, waits=ws2)
                        va = b.op("scalar", "activation",
                                  sf[0][:, :], b1.ap[:, :],
                                  AF.Silu, waits=[("pe", v1), sf_free[0]],
                                  inc="ac")
                        vt = b.op("vector", "tensor_tensor",
                                  sf[1][:, :],
                                  sf[0][:, :], b2.ap[:, :],
                                  ALU.mult,
                                  waits=[("ac", va), ("pe", v2), sf_free[1]],
                                  inc="dv")
                        war = [h_war[buf]] if j == 0 else []
                        vh = b.op("vector", "tensor_tensor",
                                  mids(8 * buf + j, 512 * h, 512 * (h + 1)),
                                  sf[1][:, :],
                                  cbb[h].ap[:, :], ALU.mult,
                                  waits=[("dv", vt), ("pe", cb_pe[h])] + war,
                                  inc="dv")
                        b1.free = ("dv", vh)
                        b2.free = ("dv", vh)
                        sf_free = [("dv", vh), ("dv", vh)]
                        h_ready = vh
                    sA.free = ("pe", b.cnt["pe"])
                    sB.free = ("pe", b.cnt["pe"])
                    if u + 3 < 64:
                        up_load5(u + 3)
                cbb[0].free = ("dv", h_ready)
                cbb[1].free = ("dv", h_ready)
                for oc in range(NCH):
                    u = 16 * e + oc
                    slot = dsl[u % 4]
                    for h in range(2):
                        bank = banks[4 + (2 * oc + h) % 4]
                        prs = [(slot.ap[:, slot.off + 128 * k:
                                        slot.off + 128 * (k + 1)],
                                mids(8 * buf + k, 512 * h, 512 * (h + 1)), [])
                               for k in range(8)]
                        ws = [slot.ready, ("dv", h_ready)]
                        if bank.free:
                            ws.append(bank.free)
                        vpe = b.chain(bank.ap[:, :], prs, waits=ws)
                        vd = acc_add(oc, h, bank.ap[:, :], vpe)
                        bank.free = ("dv", vd)
                    slot.free = ("pe", b.cnt["pe"])
                    if u + 4 < 128:
                        dn_load5(u + 4)
                    if e == NEXP - 1 and debug is None:
                        for h in range(2):
                            b.dma("sync",
                                  outT[128 * oc:128 * (oc + 1),
                                       512 * h:512 * (h + 1)],
                                  accs(oc, 512 * h, 512 * (h + 1)),
                                  waits=[("dv", acc_state[(oc, h)])])
                h_war[buf] = ("pe", b.cnt["pe"])

        if debug == "xn":
            for d in range(NCH):
                b.op("vector", "tensor_copy", accs(d),
                     xns(d, 0, 1024), waits=[("dv", v_xn_mine)], inc="dv")
        elif debug == "kvs":
            b.op("vector", "tensor_copy", acc[:, 0:HEADS * 129],
                 kvs[:, :], waits=[v_kvs_glob], inc="dv")
        elif debug == "qp":
            for d in range(NCH):
                ws = [("dv", v_qp[(d, 0)]), ("dv", v_qp[(d, 1)])]
                if attn_done:
                    ws = [("dv", attn_done[(d, 0)]), ("dv", attn_done[(d, 1)])]
                b.op("vector", "tensor_copy", accs(d), mids(d), waits=ws,
                     inc="dv")
        elif debug == "comb":
            b.op("vector", "tensor_copy", acc[0:8, 0:1024], scomb[0:8, :],
                 waits=[("dv", v_comb)], inc="dv")
        elif debug == "p0":
            for d in range(NCH):
                b.op("vector", "tensor_copy", accs(d), mids(d),
                     waits=[("dv", b.cnt["dv"])], inc="dv")

        if not ("routed" in has and debug is None):
            v_all = b.cnt["dv"]
            for d in range(NCH):
                b.dma("sync", outT[128 * d:128 * (d + 1), :], accs(d),
                      waits=[("dv", v_all)])
        v_done = b.cnt["sq"]

        block = ctx.enter_context(nc.Block())

        def runner(eng_name):
            def run(e):
                for fn in b.streams[eng_name]:
                    fn(e)
                if eng_name == "sync":
                    e.wait_ge(b.semh["sq"], v_done)
            return run

        block.sync(runner("sync"))
        block.gpsimd(runner("gpsimd"))
        block.tensor(runner("tensor"))
        block.vector(runner("vector"))
        block.scalar(runner("scalar"))

    return nc


_CACHE = {}


def _get_nc(stages, debug):
    key = (tuple(stages), debug)
    if key not in _CACHE:
        _CACHE[key] = build(stages, debug)
    return _CACHE[key]


def _prep_host(inputs):
    import ml_dtypes
    f = np.float32
    g = {}
    x = np.asarray(inputs["L0_attn_hidden_states"], f)
    xt = np.ascontiguousarray(x.transpose(0, 2, 1))

    def panels(wT, pc):
        K, O = wT.shape
        return np.ascontiguousarray(
            wT.reshape(K // 128, 128, O // pc, pc).transpose(2, 1, 0, 3)
              .reshape(O // pc, 128, (K // 128) * pc))

    g["qwP"] = panels(np.asarray(inputs["L0_attn_q_w"], f).T, 128)
    g["kwP"] = panels(np.asarray(inputs["L0_attn_k_w"], f).T, 256)
    g["vwP"] = panels(np.asarray(inputs["L0_attn_v_w"], f).T, 256)
    g["owP"] = panels(np.asarray(inputs["L0_attn_o_w"], f).T, 128)
    g["rwP"] = panels(np.asarray(inputs["L0_ffn_router_w"], f).T, 8)[0]
    g["sw1P"] = panels(np.asarray(inputs["L0_ffn_sw1"], f).T, 128)
    g["sw2P"] = panels(np.asarray(inputs["L0_ffn_sw2"], f).T, 128)
    sw3T = np.asarray(inputs["L0_ffn_sw3"], f).T
    g["sw3P"] = np.ascontiguousarray(
        sw3T.reshape(8, 8, 128, 16, 128).transpose(0, 3, 2, 1, 4)
            .reshape(128, 128, 1024))
    w12T = np.asarray(inputs["L0_ffn_w12"], f).transpose(0, 2, 1)
    g["w12P"] = np.ascontiguousarray(
        np.stack([panels(w12T[e], 128) for e in range(8)]))
    w3T = np.asarray(inputs["L0_ffn_w3"], f).transpose(0, 2, 1)
    g["w3P"] = np.ascontiguousarray(
        np.stack([panels(w3T[e], 128) for e in range(8)]))
    g["qb"] = np.asarray(inputs["L0_attn_q_b"], f).reshape(1, D)
    g["kb"] = np.asarray(inputs["L0_attn_k_b"], f).reshape(1, D)
    g["vb"] = np.asarray(inputs["L0_attn_v_b"], f).reshape(1, D)
    g["ob"] = np.asarray(inputs["L0_attn_o_b"], f).reshape(1, D)
    g["ebias"] = np.asarray(inputs["L0_ffn_expert_bias"], f).reshape(1, NEXP)
    g["eye_in"] = np.eye(128, dtype=f)
    g["ones_f32_in"] = np.ones((1, 128), f)
    g["ones_bfr_in"] = np.ones((1, 512), ml_dtypes.bfloat16)
    g["ones_bfc_in"] = np.ones((128, 1), ml_dtypes.bfloat16)
    sel = np.zeros((8, 8 * 128), ml_dtypes.bfloat16)
    for e in range(8):
        sel[e, 128 * e:128 * (e + 1)] = 1
    g["sel_in"] = sel
    in_maps = []
    for c in range(8):
        bb, hf = c // 2, c % 2
        m = dict(g)
        m["xT_mine"] = np.ascontiguousarray(
            xt[bb, :, TOK * hf:TOK * (hf + 1)])
        in_maps.append(m)
    return in_maps


def kernel(**inputs):
    nc = _get_nc(STAGES, DEBUG_OUT)
    in_maps = _prep_host(inputs)
    res = run_bass_kernel_spmd(nc, in_maps, core_ids=list(range(8)))
    out = np.empty((4, 2048, D), np.float32)
    for c in range(8):
        bb, hf = c // 2, c % 2
        out[bb, TOK * hf:TOK * (hf + 1), :] = res.results[c]["outT"].T
    return out


# version 16
# speedup vs baseline: 1.0887x; 1.0887x over previous
import numpy as np
import concourse.bass as bass
import concourse.mybir as mybir
from concourse.bass_utils import run_bass_kernel_spmd
from contextlib import ExitStack

F32 = mybir.dt.float32
BF16 = mybir.dt.bfloat16
AF = mybir.ActivationFunctionType
ALU = mybir.AluOpType
AX = mybir.AxisListType

D = 2048
NCH = 16
TOK = 1024
HEADS = 16
EPS_LN = 1e-5
EPS_ATTN = 1e-6
EPS_RT = 1e-9
NEXP = 8
HSH = 8192
HRT = 1024

ALL_STAGES = ("ln", "kv", "q", "attn", "router", "shared", "routed")
STAGES = ALL_STAGES
DEBUG_OUT = None


class Bld:
    ENGINES = ("sync", "gpsimd", "tensor", "vector", "scalar")

    def __init__(self):
        self.streams = {e: [] for e in self.ENGINES}
        self.cnt = {s: 0 for s in ("sq", "gq", "pe", "dv", "ac", "cs", "bq",
                                   "w0", "w1", "w2", "w3", "w4", "w5", "w6",
                                   "w7", "m0", "m1", "m2", "m3",
                                   "n0", "n1", "n2", "n3",
                                   "xo", "ba", "bb", "cc1", "cca", "ccb")}
        self.semh = {}
        self.lastwait = {e: {} for e in self.ENGINES}

    def emit(self, eng, fn):
        self.streams[eng].append(fn)

    def wait(self, eng, sem, val):
        if val is None or val <= 0:
            return
        if self.lastwait[eng].get(sem, 0) >= val:
            return
        self.lastwait[eng][sem] = val

        def f(e, sem=sem, val=val):
            e.wait_ge(self.semh[sem], val)
        self.emit(eng, f)

    def waits(self, eng, ws):
        for w in ws:
            if w is not None:
                self.wait(eng, w[0], w[1])

    def op(self, eng, meth, *args, waits=(), inc=None, **kwargs):
        self.waits(eng, waits)
        val = None
        if inc is not None:
            self.cnt[inc] += 1
            val = self.cnt[inc]

        def f(e, meth=meth, args=args, kwargs=kwargs, inc=inc):
            ins = getattr(e, meth)(*args, **kwargs)
            if inc is not None:
                ins.then_inc(self.semh[inc], 1)
        self.emit(eng, f)
        return val

    def dma(self, queue, out, in_, waits=(), sem=None):
        if sem is None:
            sem = {"sync": "sq", "gpsimd": "gq"}[queue]
        self.waits(queue, waits)
        self.cnt[sem] += 16
        val = self.cnt[sem]

        def f(e, out=out, in_=in_, sem=sem):
            e.dma_start(out=out, in_=in_).then_inc(self.semh[sem], 16)
        self.emit(queue, f)
        return val

    def chain(self, out, pairs, waits=(), inc_last=True):
        n = len(pairs)
        v = None
        for i, (l, r, w) in enumerate(pairs):
            ws = (tuple(waits) if i == 0 else ()) + tuple(w)
            v = self.op("tensor", "matmul", out, l, r, waits=ws,
                        inc=("pe" if (inc_last and i == n - 1) else None),
                        start=(i == 0), stop=(i == n - 1))
        return v


class Slot:
    def __init__(self, ap, sem=None):
        self.ap = ap
        self.sem = sem
        self.ready = None
        self.free = None


def build(stages=None, debug=None):
    stages = tuple(STAGES if stages is None else stages)
    debug = DEBUG_OUT if debug is None else debug
    assert stages == ALL_STAGES[:len(stages)], "stages must be a prefix"
    has = set(stages)

    nc = bass.Bass("TRN2", target_bir_lowering=False)
    b = Bld()
    dr = {}

    def din(name, shape, dt=F32):
        dr[name] = nc.dram_tensor(name, shape, dt, kind="ExternalInput").ap()
        return dr[name]

    xT_mine = din("xT_mine", [D, TOK])
    qwP = din("qwP", [16, 128, 2048]); kwP = din("kwP", [8, 128, 4096])
    vwP = din("vwP", [8, 128, 4096]); owP = din("owP", [16, 128, 2048])
    rwP = din("rwP", [128, 128])
    sw1P = din("sw1P", [64, 128, 2048]); sw2P = din("sw2P", [64, 128, 2048])
    sw3P = din("sw3P", [128, 128, 1024])
    w12P = din("w12P", [NEXP, 16, 128, 2048])
    w3P = din("w3P", [NEXP, 16, 128, 1024])
    qb = din("qb", [1, D]); kb = din("kb", [1, D])
    vb = din("vb", [1, D]); ob = din("ob", [1, D])
    ebias = din("ebias", [1, NEXP])
    eye_in = din("eye_in", [128, 128])
    ones_f32_in = din("ones_f32_in", [1, 128])
    ones_bfr_in = din("ones_bfr_in", [1, 512], BF16)
    ones_bfc_in = din("ones_bfc_in", [128, 1], BF16)
    sel_in = din("sel_in", [8, 8 * 128], BF16)
    outT = nc.dram_tensor("outT", [D, TOK], F32, kind="ExternalOutput").ap()
    cc_in = nc.dram_tensor("cc_in", [128, HEADS * 129], BF16, kind="Internal").ap()
    cc_out = nc.dram_tensor("cc_out", [128, HEADS * 129], BF16, kind="Internal").ap()

    with ExitStack() as ctx:
        def sbuf(name, shape, dt):
            return ctx.enter_context(nc.sbuf_tensor(name, shape, dt))

        acc = sbuf("acc", [128, NCH * 1024], F32)
        xn = sbuf("xn", [128, NCH * 1024], BF16)
        mid = sbuf("mid", [128, 16384], BF16)
        kvs = sbuf("kvs", [128, HEADS * 129], BF16)
        wr = [sbuf(f"wr{i}", [128, 2048], BF16) for i in range(8)]
        sf = [sbuf(f"sf{i}", [128, 512], F32) for i in range(2)]
        sfz = sbuf("sfz", [128, 512], F32)
        zrow = sbuf("zrow", [1, 1024], F32)
        zrec = sbuf("zrec", [1, 1024], BF16)
        sm = sbuf("sm", [128, 2056], F32)
        browA = sbuf("browA", [1, D], BF16)
        browB = sbuf("browB", [1, D], BF16)
        seye = sbuf("seye", [128, 128], F32)
        sones = sbuf("sones", [1, 128], F32)
        sonesbr = sbuf("sonesbr", [1, 512], BF16)
        sonesbc = sbuf("sonesbc", [128, 1], BF16)
        ssel = sbuf("ssel", [8, 8 * 128], BF16)
        sebias = sbuf("sebias", [1, NEXP], F32)
        srw = sbuf("srw", [128, NCH * NEXP], BF16)
        scomb = sbuf("scomb", [8, 1024], BF16)

        pbt = [ctx.enter_context(nc.psum_tensor(f"pb{i}", [128, 512], F32))
               for i in range(8)]
        banks = [Slot(t) for t in pbt]

        for s in ("sq", "gq", "pe", "dv", "ac", "cs", "bq",
                  "w0", "w1", "w2", "w3", "w4", "w5", "w6", "w7",
                  "m0", "m1", "m2", "m3", "n0", "n1", "n2", "n3",
                  "xo", "ba", "bb", "cc1", "cca", "ccb"):
            b.semh[s] = ctx.enter_context(nc.semaphore(s))

        SM_EB = 0
        SM_AROW = 8
        SM_ZR = 8
        SM_SC = 520
        SM_CT = 584
        SM_W = 648
        SM_BROW = 1032
        SM_M2 = 1544

        def accs(d, lo=0, hi=1024):
            return acc[:, 1024 * d + lo: 1024 * d + hi]

        def xns(d, lo, hi):
            return xn[:, 1024 * d + lo: 1024 * d + hi]

        def mids(i, lo=0, hi=1024):
            return mid[:, 1024 * i + lo: 1024 * i + hi]

        acc_state = {}

        def acc_add(oc, h, bank_ap, vpe):
            ws = [("pe", vpe)]
            if (oc, h) in acc_state:
                ws.append(("dv", acc_state[(oc, h)]))
            v = b.op("vector", "tensor_tensor", accs(oc, 512 * h, 512 * (h + 1)),
                     accs(oc, 512 * h, 512 * (h + 1)), bank_ap, ALU.add,
                     waits=ws, inc="dv")
            acc_state[(oc, h)] = v
            return v

        rt = {"ebb": None, "comb": None}

        def emit_router_init():
            eb = banks[7]
            ws = [("cs", CS_ALL)]
            if eb.free:
                ws.append(eb.free)
            vpe = b.chain(eb.ap[:, 0:8], [(sones[:, :], sebias[0:1, :], [])],
                          waits=ws)
            rt["ebb"] = b.op("vector", "tensor_copy", sm[:, SM_EB:SM_EB + 8],
                             eb.ap[:, 0:8], waits=[("pe", vpe)], inc="dv")
            eb.free = ("dv", rt["ebb"])

        def emit_router_chunk(j, xnv=None):
            W = SM_W
            rb = banks[6 + j % 2]
            ws = [("bq", v_rw), ("dv", v_xn_mine if xnv is None else xnv)]
            if rb.free:
                ws.append(rb.free)
            prs = [(xns(d, 128 * j, 128 * (j + 1)),
                    srw[:, NEXP * d:NEXP * (d + 1)], [])
                   for d in range(NCH)]
            vpe = b.chain(rb.ap[:, 0:8], prs, waits=ws)
            va = b.op("scalar", "activation",
                      sm[:, SM_SC + 8 * j:SM_SC + 8 * (j + 1)],
                      rb.ap[:, 0:8], AF.Sigmoid,
                      waits=[("pe", vpe)], inc="ac")
            rb.free = ("ac", va)
            sc = sm[:, SM_SC + 8 * j:SM_SC + 8 * (j + 1)]
            sel_t = sm[:, W:W + 8]
            m1 = sm[:, W + 32:W + 33]
            m2 = sm[:, W + 33:W + 34]
            ssum = sm[:, W + 34:W + 35]
            rec = sm[:, W + 35:W + 36]
            mask1 = sm[:, W + 8:W + 16]
            mask2 = sm[:, W + 16:W + 24]
            ms = sm[:, W + 24:W + 32]
            v1 = b.op("vector", "tensor_tensor", sel_t, sc,
                      sm[:, SM_EB:SM_EB + 8], ALU.add,
                      waits=[("ac", va), ("dv", rt["ebb"])], inc="dv")
            v2 = b.op("vector", "tensor_reduce", m1, sel_t, AX.X, ALU.max,
                      waits=[("dv", v1)], inc="dv")
            v3 = b.op("vector", "tensor_scalar", mask1, sel_t, m1, None,
                      ALU.is_ge, waits=[("dv", v2)], inc="dv")
            v4 = b.op("vector", "scalar_tensor_tensor", sm[:, W:W + 8],
                      mask1, -1e9, sel_t, ALU.mult, ALU.add,
                      waits=[("dv", v3)], inc="dv")
            v5 = b.op("vector", "tensor_reduce", m2, sm[:, W:W + 8], AX.X,
                      ALU.max, waits=[("dv", v4)], inc="dv")
            v6 = b.op("vector", "tensor_scalar", mask2, sm[:, W:W + 8], m2,
                      None, ALU.is_ge, waits=[("dv", v5)], inc="dv")
            v7 = b.op("vector", "tensor_tensor", mask1, mask1, mask2,
                      ALU.add, waits=[("dv", v6)], inc="dv")
            v8 = b.op("vector", "tensor_tensor", ms, mask1, sc, ALU.mult,
                      waits=[("dv", v7)], inc="dv")
            v9 = b.op("vector", "tensor_reduce", ssum, ms, AX.X, ALU.add,
                      waits=[("dv", v8)], inc="dv")
            v10 = b.op("vector", "tensor_scalar", ssum, ssum,
                       float(EPS_RT), None, ALU.add,
                       waits=[("dv", v9)], inc="dv")
            v11 = b.op("vector", "reciprocal", rec, ssum,
                       waits=[("dv", v10)], inc="dv")
            v12 = b.op("vector", "tensor_scalar",
                       sm[:, SM_CT + 8 * j:SM_CT + 8 * (j + 1)], ms, rec,
                       None, ALU.mult, waits=[("dv", v11)], inc="dv")
            rt[("ct", j)] = v12

        def emit_router_xpose(j):
            tb = banks[6 + (j + 1) % 2]
            ws = [("dv", rt[("ct", j)]), ("cs", CS_ALL)]
            if tb.free:
                ws.append(tb.free)
            vtp = b.op("tensor", "transpose", tb.ap[0:8, 0:128],
                       sm[:, SM_CT + 8 * j:SM_CT + 8 * (j + 1)],
                       seye[:, :], waits=ws, inc="pe")
            vcp = b.op("vector", "tensor_copy",
                       scomb[0:8, 128 * j:128 * (j + 1)], tb.ap[0:8, 0:128],
                       waits=[("pe", vtp)], inc="dv")
            tb.free = ("dv", vcp)
            rt["comb"] = vcp

        b.dma("sync", seye[:, :], eye_in, sem="cs")
        b.dma("sync", sonesbc[:, :], ones_bfc_in, sem="cs")
        b.dma("sync", sones[:, :], ones_f32_in, sem="cs")
        b.dma("sync", sonesbr[:, :], ones_bfr_in, sem="cs")
        b.dma("sync", sebias[:, :], ebias, sem="cs")
        CS_ALL = b.dma("sync", ssel[:, :], sel_in, sem="cs")
        v_rw = b.dma("gpsimd", srw[:, :], rwP, sem="bq")

        sf_free = [None, None]

        def ln_half():
            nonlocal sf_free
            sum_bank = [banks[4], banks[5]]
            sq_bank = [banks[6], banks[7]]
            xb_free = [None, None]
            xq_free = [None, None]
            vmark = {}
            xld = {}
            for d in range(4):
                xld[d] = b.dma("sync" if d % 2 == 0 else "gpsimd", accs(d),
                               xT_mine[128 * d:128 * (d + 1), :],
                               sem=f"m{d % 4}")
            for d in range(NCH):
                src = accs(d)
                xb_t = mids(d % 2)
                xq_t = mids(2 + d % 2)
                va = b.op("scalar", "activation", xb_t, src, AF.Identity,
                          waits=[(f"m{d % 4}", xld[d]), xb_free[d % 2]],
                          inc="ac")
                xb_rdy = ("ac", va)
                va2 = b.op("scalar", "activation", xq_t, src, AF.Square,
                           waits=[xb_rdy, xq_free[d % 2]], inc="ac")
                if d + 4 < NCH:
                    xld[d + 4] = b.dma("sync" if d % 2 == 0 else "gpsimd",
                                       accs(d + 4),
                                       xT_mine[128 * (d + 4):128 * (d + 5), :],
                                       sem=f"m{d % 4}", waits=[("ac", va2)])
                for i, (bankset, tile, w) in enumerate(
                        ((sum_bank, xb_t, xb_rdy),
                         (sq_bank, xq_t, ("ac", va2)))):
                    for h in range(2):
                        bank = bankset[h]
                        ws = [w, ("cs", CS_ALL)]
                        if d == 0 and bank.free:
                            ws.append(bank.free)
                        last = (i == 1 and h == 1)
                        v = b.op("tensor", "matmul", bank.ap[0:1, :],
                                 sonesbc[:, :], tile[:, 512 * h:512 * (h + 1)],
                                 waits=ws,
                                 inc=("pe" if last else None),
                                 start=(d == 0), stop=(d == NCH - 1))
                vmark[d] = v
                xb_free[d % 2] = ("pe", v)
                xq_free[d % 2] = ("pe", v)
            va_rows = []
            for h in range(2):
                srow = sum_bank[h].ap[0:1, :]
                qrow = sq_bank[h].ap[0:1, :]
                wv = [("pe", vmark[NCH - 1])]
                v1 = b.op("vector", "tensor_scalar", srow, srow, 1.0 / D, None,
                          ALU.mult, waits=wv, inc="dv")
                v2 = b.op("scalar", "activation", sm[0:1, SM_M2:SM_M2 + 512],
                          srow, AF.Square, waits=[("dv", v1)], inc="ac")
                v3 = b.op("vector", "scalar_tensor_tensor", qrow, qrow, 1.0 / D,
                          sm[0:1, SM_M2:SM_M2 + 512], ALU.mult, ALU.subtract,
                          waits=[("ac", v2)], inc="dv")
                v3b = b.op("vector", "tensor_scalar", qrow, qrow,
                           float(EPS_LN), None, ALU.add,
                           waits=[("dv", v3)], inc="dv")
                v4 = b.op("scalar", "activation", qrow, qrow, AF.Sqrt,
                          waits=[("dv", v3b)], inc="ac")
                v5 = b.op("vector", "reciprocal",
                          sm[0:1, SM_AROW + 512 * h: SM_AROW + 512 * (h + 1)],
                          qrow, waits=[("ac", v4)], inc="dv")
                v6 = b.op("vector", "tensor_tensor",
                          sm[0:1, SM_BROW + 512 * h: SM_BROW + 512 * (h + 1)],
                          srow,
                          sm[0:1, SM_AROW + 512 * h: SM_AROW + 512 * (h + 1)],
                          ALU.mult, waits=[("dv", v5)], inc="dv")
                va_rows.append(v6)
                sum_bank[h].free = ("dv", v6)
                sq_bank[h].free = ("dv", v6)
            ab_pe = {}
            for h in range(2):
                for k, base in enumerate((SM_AROW, SM_BROW)):
                    bank = banks[2 * k + h]
                    ws = [("dv", va_rows[h]), ("cs", CS_ALL)]
                    if bank.free:
                        ws.append(bank.free)
                    ab_pe[(k, h)] = b.chain(
                        bank.ap[:, :],
                        [(sones[:, :],
                          sm[0:1, base + 512 * h: base + 512 * (h + 1)], [])],
                        waits=ws)
            lastv = None
            halfv = [None, None]
            sf0_war = sf_free[0]
            for h in range(2):
                for d in range(NCH):
                    src = accs(d, 512 * h, 512 * (h + 1))
                    ws = [("pe", ab_pe[(0, h)]), ("pe", ab_pe[(1, h)]), sf0_war]
                    v1 = b.op("vector", "tensor_tensor",
                              sf[0][:, 0:512],
                              src, banks[h].ap[:, :], ALU.mult,
                              waits=ws, inc="dv")
                    v2 = b.op("vector", "tensor_tensor",
                              xns(d, 512 * h, 512 * (h + 1)),
                              sf[0][:, 0:512],
                              banks[2 + h].ap[:, :], ALU.subtract,
                              waits=[("dv", v1)], inc="dv")
                    sf0_war = ("dv", v2)
                    lastv = v2
                halfv[h] = v2
            for i in range(4):
                banks[i].free = ("dv", lastv)
            return halfv

        v_xn_mine = v_xn_h0 = None
        if "ln" in has:
            v_xn_h0, v_xn_mine = ln_half()
            sf_free = [("dv", v_xn_mine), sf_free[1]]
        v_pe_ln = b.cnt["pe"]

        v_kvs_glob = None
        v_kv_pe = None
        wr_free_kv = [None] * 4
        wr_free_q = [None] * 4
        if "kv" in has:
            v_kb = b.dma("gpsimd", browA[0:1, :], kb, sem="ba")
            v_vb = b.dma("gpsimd", browB[0:1, :], vb, sem="bb")
            kgrp = mid[:, 0:2048]
            vgrp = mid[:, 2048:4096]
            wpair = [
                [Slot(wr[0], "w0"), Slot(wr[1], "w1"),
                 Slot(wr[2], "w2"), Slot(wr[3], "w3")],
                [Slot(mid[:, 8192 + 2048 * i:8192 + 2048 * (i + 1)], f"n{i}")
                 for i in range(4)],
            ]
            prev_kv_pe = None
            v_kvs_done = None
            hg0_last = None
            qpre = []
            if "q" in has:
                for i, sname in enumerate(("xo", "gq")):
                    s = Slot(mid, sname)
                    s.off = 4096 + 2048 * i
                    qpre.append(s)
            if "router" in has:
                emit_router_init()
                emit_router_chunk(0, v_xn_h0)
                emit_router_chunk(1, v_xn_h0)
                emit_router_xpose(0)
                emit_router_chunk(2, v_xn_h0)
                emit_router_xpose(1)
                emit_router_chunk(3, v_xn_h0)
                emit_router_xpose(2)
            for hg in range(8):
                pair = wpair[hg % 2]
                kt = pair[0:2]
                vt = pair[2:4]
                for tiles, w_dram, queue in ((kt, kwP, "gpsimd"),
                                             (vt, vwP, "gpsimd")):
                    for half in range(2):
                        sl = tiles[half]
                        ws = [sl.free] if sl.free else []
                        if hg == 1:
                            ws.append(hg0_last)
                        v = b.dma(queue, sl.ap[:, :],
                                  w_dram[hg, :, 2048 * half:2048 * (half + 1)],
                                  waits=ws, sem=sl.sem)
                        sl.ready = (sl.sem, v)
                if hg == 0:
                    hg0_last = ("w3", b.cnt["w3"])
                if hg == 5 and "q" in has:
                    for i, sl in enumerate(qpre):
                        v = b.dma("gpsimd",
                                  sl.ap[:, sl.off:sl.off + 2048],
                                  qwP[i, :, :], sem=sl.sem)
                        sl.ready = (sl.sem, v)
                grp_last = None
                for which, (tiles, grp, brow, bsem, vbias) in enumerate(
                        ((kt, kgrp, browA, "ba", v_kb),
                         (vt, vgrp, browB, "bb", v_vb))):
                    for pk in range(4):
                        bank = banks[2 * which + pk % 2]
                        vpe = None
                        for sub in range(2):
                            j = 2 * pk + sub
                            prs = []
                            for d in range(NCH):
                                sl = tiles[d // 8]
                                prs.append((xns(d, 128 * j, 128 * (j + 1)),
                                            sl.ap[:, 256 * (d % 8):
                                                  256 * (d % 8) + 256],
                                            [sl.ready]))
                            prs.append((sonesbr[0:1, 0:128],
                                        brow[0:1, 256 * hg:256 * (hg + 1)],
                                        [(bsem, vbias), ("cs", CS_ALL)]))
                            ws = [("dv", v_xn_h0 if j < 4 else v_xn_mine)]
                            if sub == 0 and bank.free:
                                ws.append(bank.free)
                            vpe = b.chain(
                                bank.ap[:, 256 * sub:256 * (sub + 1)], prs,
                                waits=ws, inc_last=(sub == 1))
                        war = []
                        if prev_kv_pe and pk == 0:
                            war = [("pe", prev_kv_pe)]
                        elif prev_kv_pe is None and pk == 0 and which == 0:
                            war = [("pe", v_pe_ln)]
                        if which == 0:
                            va1 = b.op("scalar", "activation", sf[0][:, :],
                                       bank.ap[:, :], AF.Relu,
                                       waits=[("pe", vpe), sf_free[0]], inc="ac")
                            va2 = b.op("scalar", "activation", sf[1][:, :],
                                       bank.ap[:, :], AF.Relu, scale=-1.0,
                                       waits=[sf_free[1]], inc="ac")
                            va3 = b.op("scalar", "activation", sf[1][:, :],
                                       sf[1][:, :], AF.Exp, scale=-1.0,
                                       waits=[("ac", va2)], inc="ac")
                            vd = b.op("vector", "tensor_tensor",
                                      grp[:, 512 * pk:512 * (pk + 1)],
                                      sf[1][:, :], sf[0][:, :], ALU.add,
                                      waits=[("ac", va1), ("ac", va3)] + war,
                                      inc="dv")
                            sf_free = [("dv", vd), ("dv", vd)]
                        else:
                            vd = b.op("vector", "tensor_copy",
                                      grp[:, 512 * pk:512 * (pk + 1)],
                                      bank.ap[:, :],
                                      waits=[("pe", vpe)] + war, inc="dv")
                        bank.free = ("dv", vd)
                        grp_last = vd
                for t in range(2):
                    hh = 2 * hg + t
                    kvb = banks[4 + hh % 2]
                    ws = [("dv", grp_last)]
                    if kvb.free:
                        ws.append(kvb.free)
                    prs = [(kgrp[:, 256 * j + 128 * t:256 * j + 128 * (t + 1)],
                            vgrp[:, 256 * j + 128 * t:256 * j + 128 * (t + 1)],
                            []) for j in range(8)]
                    b.chain(kvb.ap[:, 0:128], prs, waits=ws, inc_last=False)
                    prs2 = [(kgrp[:, 256 * j + 128 * t:256 * j + 128 * (t + 1)],
                             sonesbc[:, :], []) for j in range(8)]
                    vpe2 = b.chain(kvb.ap[:, 128:129], prs2)
                    prev_kv_pe = vpe2
                    vd = b.op("vector", "tensor_copy",
                              kvs[:, 129 * hh:129 * (hh + 1)], kvb.ap[:, 0:129],
                              waits=[("pe", vpe2)], inc="dv")
                    kvb.free = ("dv", vd)
                    v_kvs_done = vd
                for sl in pair:
                    sl.free = ("pe", prev_kv_pe)
                if "router" in has and hg <= 4:
                    if hg <= 3:
                        emit_router_chunk(4 + hg)
                    emit_router_xpose(3 + hg)
            v_kv_pe = prev_kv_pe
            b.dma("sync", cc_in, kvs[:, :], sem="cca",
                  waits=[("dv", v_kvs_done)])
            wr_free_kv = [sl.free for sl in wpair[0]]

        def emit_collective():
            b.op("gpsimd", "collective_compute",
                 "AllReduce", ALU.add,
                 [[0, 1], [2, 3], [4, 5], [6, 7]],
                 [cc_in.opt()], [cc_out.opt()],
                 waits=[("cca", 16)], inc="cc1")
            b.dma("sync", kvs[:, :], cc_out, sem="ccb",
                  waits=[("cc1", 1)])
            return ("ccb", 16)

        if "kv" in has and "q" not in has:
            v_kvs_glob = emit_collective()

        v_qp = {}
        v_q_pe = None
        attn_done = {}
        v_attn_pe = None
        do_attn = "attn" in has
        zrow_free = {0: None, 1: None}
        zrec_free = {0: None, 1: None}
        sfz_free = [None]

        vrec_s = {}
        vat_s = {}

        def attn_stageA(i):
            for h in range(2):
                zb = banks[4 + h]
                ws = [v_kvs_glob, ("dv", v_qp[(i, h)])]
                if zb.free:
                    ws.append(zb.free)
                vz = b.chain(zb.ap[0:1, :],
                             [(kvs[:, 129 * i + 128:129 * (i + 1)],
                               mids(i, 512 * h, 512 * (h + 1)), [])],
                             waits=ws)
                ws2 = [("pe", vz), zrow_free[h]]
                v1 = b.op("vector", "tensor_copy",
                          zrow[0:1, 512 * h:512 * (h + 1)], zb.ap[0:1, :],
                          waits=ws2, inc="dv")
                zb.free = ("dv", v1)
                v2 = b.op("vector", "tensor_scalar",
                          zrow[0:1, 512 * h:512 * (h + 1)],
                          zrow[0:1, 512 * h:512 * (h + 1)],
                          float(EPS_ATTN), None, ALU.add,
                          waits=[("dv", v1)], inc="dv")
                v3 = b.op("vector", "reciprocal",
                          zrow[0:1, 512 * h:512 * (h + 1)],
                          zrow[0:1, 512 * h:512 * (h + 1)],
                          waits=[("dv", v2)], inc="dv")
                v3b = b.op("vector", "tensor_copy",
                           zrec[0:1, 512 * h:512 * (h + 1)],
                           zrow[0:1, 512 * h:512 * (h + 1)],
                           waits=[("dv", v3), zrec_free[h]], inc="dv")
                zrow_free[h] = ("dv", v3b)
                vrec_s[(i, h)] = v3b
            for h in range(2):
                ab = banks[(2 * i + h) % 4]
                ws = [("dv", v_qp[(i, h)])]
                if ab.free:
                    ws.append(ab.free)
                vat_s[(i, h)] = b.chain(ab.ap[:, :],
                                        [(kvs[:, 129 * i:129 * i + 128],
                                          mids(i, 512 * h, 512 * (h + 1)), [])],
                                        waits=ws)

        def attn_stageB(i):
            vbc = {}
            for h in range(2):
                bc = banks[6 + h]
                ws = [("dv", vrec_s[(i, h)]), ("cs", CS_ALL)]
                if bc.free:
                    ws.append(bc.free)
                vbc[h] = b.chain(bc.ap[:, :],
                                 [(sonesbr[0:1, 0:128],
                                   zrec[0:1, 512 * h:512 * (h + 1)], [])],
                                 waits=ws)
                zrec_free[h] = ("pe", vbc[h])
            for h in range(2):
                bc = banks[6 + h]
                ab = banks[(2 * i + h) % 4]
                v4 = b.op("vector", "tensor_copy", sfz[:, :], bc.ap[:, :],
                          waits=[("pe", vbc[h]), sfz_free[0]], inc="dv")
                bc.free = ("dv", v4)
                vd = b.op("vector", "tensor_tensor",
                          mids(i, 512 * h, 512 * (h + 1)),
                          sfz[:, :], ab.ap[:, :], ALU.mult,
                          waits=[("pe", vat_s[(i, h)]), ("dv", v4)], inc="dv")
                sfz_free[0] = ("dv", vd)
                ab.free = ("dv", vd)
                attn_done[(i, h)] = vd

        if "q" in has:
            ring = [Slot(wr[i], f"w{i}") for i in range(4)]
            for i, s in enumerate(ring):
                s.free = wr_free_kv[i] if "kv" in has else None

            def q_load(oc):
                slot = ring[oc % 4]
                ws = [slot.free] if slot.free else []
                v = b.dma("gpsimd", slot.ap[:, :], qwP[oc, :, :], waits=ws,
                          sem=slot.sem)
                slot.ready = (slot.sem, v)

            for oc in range(2, 4):
                q_load(oc)
            v_qb = b.dma("gpsimd", browA[0:1, :], qb, sem="ba",
                         waits=[("pe", v_kv_pe)] if v_kv_pe else [])
            if "kv" in has:
                v_kvs_glob = emit_collective()
            for oc in range(NCH):
                if do_attn and oc >= 6:
                    attn_stageB(oc - 6)
                if do_attn and oc >= 5:
                    attn_stageA(oc - 5)
                slot = qpre[oc] if (oc < 2 and qpre) else ring[oc % 4]
                off = getattr(slot, "off", 0)
                for h in range(2):
                    bank = banks[(2 * oc + h) % 4]
                    prs = [(slot.ap[:, off + 128 * d:off + 128 * (d + 1)],
                            xns(d, 512 * h, 512 * (h + 1)), [])
                           for d in range(NCH)]
                    prs.append((browA[0:1, 128 * oc:128 * (oc + 1)],
                                sonesbr[0:1, :], [("ba", v_qb), ("cs", CS_ALL)]))
                    ws = [slot.ready, ("dv", v_xn_mine)]
                    if bank.free:
                        ws.append(bank.free)
                    vpe = b.chain(bank.ap[:, :], prs, waits=ws)
                    war = ([("pe", v_kv_pe)]
                           if (v_kv_pe and oc < 4) else [])
                    va1 = b.op("scalar", "activation", sf[0][:, :],
                               bank.ap[:, :], AF.Relu,
                               waits=[("pe", vpe), sf_free[0]], inc="ac")
                    va2 = b.op("scalar", "activation", sf[1][:, :],
                               bank.ap[:, :], AF.Relu, scale=-1.0,
                               waits=[sf_free[1]], inc="ac")
                    va3 = b.op("scalar", "activation", sf[1][:, :],
                               sf[1][:, :], AF.Exp, scale=-1.0,
                               waits=[("ac", va2)], inc="ac")
                    vd = b.op("vector", "tensor_tensor",
                              mids(oc, 512 * h, 512 * (h + 1)),
                              sf[1][:, :], sf[0][:, :], ALU.add,
                              waits=[("ac", va1), ("ac", va3)] + war, inc="dv")
                    sf_free = [("dv", vd), ("dv", vd)]
                    bank.free = ("dv", vd)
                    v_qp[(oc, h)] = vd
                slot.free = ("pe", b.cnt["pe"])
                if oc + 4 < NCH:
                    q_load(oc + 4)
            if do_attn:
                attn_stageB(10)
                for i in range(11, HEADS):
                    attn_stageA(i)
                    attn_stageB(i)
            v_q_pe = b.cnt["pe"]
            wr_free_q = [ring[i].free for i in range(4)]

        v_comb = rt["comb"]

        if "attn" in has:
            v_ob = b.dma("gpsimd", browB[0:1, :], ob, sem="bb",
                         waits=[("pe", v_kv_pe)] if v_kv_pe else [])
            ring = [Slot(wr[i], f"w{i}") for i in range(4)]
            for i, s in enumerate(ring):
                s.free = wr_free_q[i] if "q" in has else None

            def o_load(oc):
                slot = ring[oc % 4]
                ws = [slot.free] if slot.free else []
                v = b.dma("gpsimd", slot.ap[:, :], owP[oc, :, :], waits=ws,
                          sem=slot.sem)
                slot.ready = (slot.sem, v)

            for oc in range(4):
                o_load(oc)
            v_attn_all = attn_done[(HEADS - 1, 1)]
            for oc in range(NCH):
                slot = ring[oc % 4]
                for h in range(2):
                    bank = banks[(2 * oc + h) % 4]
                    prs = [(slot.ap[:, 128 * hh2:128 * (hh2 + 1)],
                            mids(hh2, 512 * h, 512 * (h + 1)), [])
                           for hh2 in range(HEADS)]
                    prs.append((browB[0:1, 128 * oc:128 * (oc + 1)],
                                sonesbr[0:1, :], [("bb", v_ob), ("cs", CS_ALL)]))
                    ws = [slot.ready, ("dv", v_attn_all)]
                    if bank.free:
                        ws.append(bank.free)
                    vpe = b.chain(bank.ap[:, :], prs, waits=ws)
                    vd = acc_add(oc, h, bank.ap[:, :], vpe)
                    bank.free = ("dv", vd)
                slot.free = ("pe", b.cnt["pe"])
                if oc + 4 < NCH:
                    o_load(oc + 4)
            v_attn_pe = b.cnt["pe"]
            wr_free_o = [ring[i].free for i in range(4)]

        uring = [Slot(wr[i], f"w{i}") for i in range(6)]
        for i, sl in enumerate(uring):
            sl.free = wr_free_o[i] if (i < 4 and "attn" in has) else None
        dsl = []
        for i, (t, off) in enumerate(((wr[6], 0), (wr[6], 1024),
                                      (wr[7], 0), (wr[7], 1024))):
            s = Slot(t, f"w{6 + i // 2}")
            s.off = off
            dsl.append(s)

        if "shared" in has:
            mid_war = {0: ("pe", v_attn_pe) if v_attn_pe else None,
                       1: ("pe", v_attn_pe) if v_attn_pe else None}

            def up_load4(u):
                p = u % 3
                for slot, wd in ((uring[2 * p], sw1P), (uring[2 * p + 1], sw2P)):
                    ws = [slot.free] if slot.free else []
                    v = b.dma("gpsimd", slot.ap[:, :], wd[u, :, :],
                              waits=ws, sem=slot.sem)
                    slot.ready = (slot.sem, v)

            def dn_load4(u):
                slot = dsl[u % 4]
                ws = [slot.free] if slot.free else []
                v = b.dma("gpsimd", slot.ap[:, slot.off:slot.off + 1024],
                          sw3P[u, :, :], waits=ws, sem=slot.sem)
                slot.ready = (slot.sem, v)

            for u in range(3):
                up_load4(u)
            for u in range(4):
                dn_load4(u)
            for g in range(8):
                buf = g % 2
                p_last = None
                for hc in range(8):
                    u = 8 * g + hc
                    p = u % 3
                    sA, sB = uring[2 * p], uring[2 * p + 1]
                    for h in range(2):
                        bg = banks[h]
                        bu = banks[2 + h]
                        prsg = [(sA.ap[:, 128 * d:128 * (d + 1)],
                                 xns(d, 512 * h, 512 * (h + 1)), [sA.ready])
                                for d in range(NCH)]
                        prsu = [(sB.ap[:, 128 * d:128 * (d + 1)],
                                 xns(d, 512 * h, 512 * (h + 1)), [sB.ready])
                                for d in range(NCH)]
                        wsg = [("dv", v_xn_mine)]
                        wsu = []
                        if bg.free:
                            wsg.append(bg.free)
                        if bu.free:
                            wsu.append(bu.free)
                        vg = b.chain(bg.ap[:, :], prsg, waits=wsg)
                        vu = b.chain(bu.ap[:, :], prsu, waits=wsu)
                        va = b.op("scalar", "activation",
                                  sf[0][:, :], bg.ap[:, :], AF.Silu,
                                  waits=[("pe", vg), sf_free[0]], inc="ac")
                        war = [mid_war[buf]] if hc == 0 and h == 0 else []
                        vd = b.op("vector", "tensor_tensor",
                                  mids(8 * buf + hc, 512 * h, 512 * (h + 1)),
                                  sf[0][:, :], bu.ap[:, :], ALU.mult,
                                  waits=[("ac", va), ("pe", vu)] + war,
                                  inc="dv")
                        sf_free[0] = ("dv", vd)
                        bg.free = ("dv", vd)
                        bu.free = ("dv", vd)
                        p_last = vd
                    sA.free = ("pe", b.cnt["pe"])
                    sB.free = ("pe", b.cnt["pe"])
                    if u + 3 < 64:
                        up_load4(u + 3)
                for oc in range(NCH):
                    u = 16 * g + oc
                    slot = dsl[u % 4]
                    for h in range(2):
                        bank = banks[4 + (2 * oc + h) % 4]
                        prs = [(slot.ap[:, slot.off + 128 * k:
                                        slot.off + 128 * (k + 1)],
                                mids(8 * buf + k, 512 * h, 512 * (h + 1)),
                                []) for k in range(8)]
                        ws = [slot.ready, ("dv", p_last)]
                        if bank.free:
                            ws.append(bank.free)
                        vpe = b.chain(bank.ap[:, :], prs, waits=ws)
                        vd = acc_add(oc, h, bank.ap[:, :], vpe)
                        bank.free = ("dv", vd)
                    slot.free = ("pe", b.cnt["pe"])
                    if u + 4 < 128:
                        dn_load4(u + 4)
                mid_war[buf] = ("pe", b.cnt["pe"])
            mid_war_s = dict(mid_war)

        if "routed" in has:
            h_war = (dict(mid_war_s) if "shared" in has
                     else {0: ("pe", b.cnt["pe"]), 1: ("pe", b.cnt["pe"])})

            def up_load5(u):
                e, j = divmod(u, 8)
                p = u % 3
                for slot, jj in ((uring[2 * p], j), (uring[2 * p + 1], 8 + j)):
                    ws = [slot.free] if slot.free else []
                    v = b.dma("gpsimd", slot.ap[:, :], w12P[e, jj, :, :],
                              waits=ws, sem=slot.sem)
                    slot.ready = (slot.sem, v)

            def dn_load5(u):
                e, oc = divmod(u, 16)
                slot = dsl[u % 4]
                ws = [slot.free] if slot.free else []
                v = b.dma("gpsimd", slot.ap[:, slot.off:slot.off + 1024],
                          w3P[e, oc, :, :], waits=ws, sem=slot.sem)
                slot.ready = (slot.sem, v)

            for u in range(3):
                up_load5(u)
            for u in range(4):
                dn_load5(u)
            for e in range(NEXP):
                buf = e % 2
                cbb = [banks[6], banks[7]]
                cb_pe = {}
                for h in range(2):
                    ws = [("dv", v_comb), ("cs", CS_ALL)]
                    if cbb[h].free:
                        ws.append(cbb[h].free)
                    cb_pe[h] = b.chain(
                        cbb[h].ap[:, :],
                        [(ssel[:, 128 * e:128 * (e + 1)],
                          scomb[0:8, 512 * h:512 * (h + 1)], [])], waits=ws)
                h_ready = None
                for j in range(8):
                    u = 8 * e + j
                    p = u % 3
                    sA, sB = uring[2 * p], uring[2 * p + 1]
                    for h in range(2):
                        b1 = banks[h]
                        b2 = banks[2 + h]
                        prs1 = [(sA.ap[:, 128 * d:128 * (d + 1)],
                                 xns(d, 512 * h, 512 * (h + 1)), [sA.ready])
                                for d in range(NCH)]
                        prs2 = [(sB.ap[:, 128 * d:128 * (d + 1)],
                                 xns(d, 512 * h, 512 * (h + 1)), [sB.ready])
                                for d in range(NCH)]
                        ws1 = []
                        ws2 = []
                        if b1.free:
                            ws1.append(b1.free)
                        if b2.free:
                            ws2.append(b2.free)
                        v1 = b.chain(b1.ap[:, :], prs1, waits=ws1)
                        v2 = b.chain(b2.ap[:, :], prs2, waits=ws2)
                        va = b.op("scalar", "activation",
                                  sf[0][:, :], b1.ap[:, :],
                                  AF.Silu, waits=[("pe", v1), sf_free[0]],
                                  inc="ac")
                        vt = b.op("vector", "tensor_tensor",
                                  sf[1][:, :],
                                  sf[0][:, :], b2.ap[:, :],
                                  ALU.mult,
                                  waits=[("ac", va), ("pe", v2), sf_free[1]],
                                  inc="dv")
                        war = [h_war[buf]] if j == 0 else []
                        vh = b.op("vector", "tensor_tensor",
                                  mids(8 * buf + j, 512 * h, 512 * (h + 1)),
                                  sf[1][:, :],
                                  cbb[h].ap[:, :], ALU.mult,
                                  waits=[("dv", vt), ("pe", cb_pe[h])] + war,
                                  inc="dv")
                        b1.free = ("dv", vh)
                        b2.free = ("dv", vh)
                        sf_free = [("dv", vh), ("dv", vh)]
                        h_ready = vh
                    sA.free = ("pe", b.cnt["pe"])
                    sB.free = ("pe", b.cnt["pe"])
                    if u + 3 < 64:
                        up_load5(u + 3)
                cbb[0].free = ("dv", h_ready)
                cbb[1].free = ("dv", h_ready)
                for oc in range(NCH):
                    u = 16 * e + oc
                    slot = dsl[u % 4]
                    for h in range(2):
                        bank = banks[4 + (2 * oc + h) % 4]
                        prs = [(slot.ap[:, slot.off + 128 * k:
                                        slot.off + 128 * (k + 1)],
                                mids(8 * buf + k, 512 * h, 512 * (h + 1)), [])
                               for k in range(8)]
                        ws = [slot.ready, ("dv", h_ready)]
                        if bank.free:
                            ws.append(bank.free)
                        vpe = b.chain(bank.ap[:, :], prs, waits=ws)
                        vd = acc_add(oc, h, bank.ap[:, :], vpe)
                        bank.free = ("dv", vd)
                    slot.free = ("pe", b.cnt["pe"])
                    if u + 4 < 128:
                        dn_load5(u + 4)
                    if e == NEXP - 1 and debug is None:
                        for h in range(2):
                            b.dma("sync",
                                  outT[128 * oc:128 * (oc + 1),
                                       512 * h:512 * (h + 1)],
                                  accs(oc, 512 * h, 512 * (h + 1)),
                                  waits=[("dv", acc_state[(oc, h)])])
                h_war[buf] = ("pe", b.cnt["pe"])

        if debug == "xn":
            for d in range(NCH):
                b.op("vector", "tensor_copy", accs(d),
                     xns(d, 0, 1024), waits=[("dv", v_xn_mine)], inc="dv")
        elif debug == "kvs":
            b.op("vector", "tensor_copy", acc[:, 0:HEADS * 129],
                 kvs[:, :], waits=[v_kvs_glob], inc="dv")
        elif debug == "qp":
            for d in range(NCH):
                ws = [("dv", v_qp[(d, 0)]), ("dv", v_qp[(d, 1)])]
                if attn_done:
                    ws = [("dv", attn_done[(d, 0)]), ("dv", attn_done[(d, 1)])]
                b.op("vector", "tensor_copy", accs(d), mids(d), waits=ws,
                     inc="dv")
        elif debug == "comb":
            b.op("vector", "tensor_copy", acc[0:8, 0:1024], scomb[0:8, :],
                 waits=[("dv", v_comb)], inc="dv")
        elif debug == "p0":
            for d in range(NCH):
                b.op("vector", "tensor_copy", accs(d), mids(d),
                     waits=[("dv", b.cnt["dv"])], inc="dv")

        if not ("routed" in has and debug is None):
            v_all = b.cnt["dv"]
            for d in range(NCH):
                b.dma("sync", outT[128 * d:128 * (d + 1), :], accs(d),
                      waits=[("dv", v_all)])
        v_done = b.cnt["sq"]

        block = ctx.enter_context(nc.Block())

        def runner(eng_name):
            def run(e):
                for fn in b.streams[eng_name]:
                    fn(e)
                if eng_name == "sync":
                    e.wait_ge(b.semh["sq"], v_done)
            return run

        block.sync(runner("sync"))
        block.gpsimd(runner("gpsimd"))
        block.tensor(runner("tensor"))
        block.vector(runner("vector"))
        block.scalar(runner("scalar"))

    return nc


_CACHE = {}


def _get_nc(stages, debug):
    key = (tuple(stages), debug)
    if key not in _CACHE:
        _CACHE[key] = build(stages, debug)
    return _CACHE[key]


def _prep_host(inputs):
    import ml_dtypes
    f = np.float32
    g = {}
    x = np.asarray(inputs["L0_attn_hidden_states"], f)
    xt = np.ascontiguousarray(x.transpose(0, 2, 1))

    def panels(wT, pc):
        K, O = wT.shape
        return np.ascontiguousarray(
            wT.reshape(K // 128, 128, O // pc, pc).transpose(2, 1, 0, 3)
              .reshape(O // pc, 128, (K // 128) * pc))

    g["qwP"] = panels(np.asarray(inputs["L0_attn_q_w"], f).T, 128)
    g["kwP"] = panels(np.asarray(inputs["L0_attn_k_w"], f).T, 256)
    g["vwP"] = panels(np.asarray(inputs["L0_attn_v_w"], f).T, 256)
    g["owP"] = panels(np.asarray(inputs["L0_attn_o_w"], f).T, 128)
    g["rwP"] = panels(np.asarray(inputs["L0_ffn_router_w"], f).T, 8)[0]
    g["sw1P"] = panels(np.asarray(inputs["L0_ffn_sw1"], f).T, 128)
    g["sw2P"] = panels(np.asarray(inputs["L0_ffn_sw2"], f).T, 128)
    sw3T = np.asarray(inputs["L0_ffn_sw3"], f).T
    g["sw3P"] = np.ascontiguousarray(
        sw3T.reshape(8, 8, 128, 16, 128).transpose(0, 3, 2, 1, 4)
            .reshape(128, 128, 1024))
    w12T = np.asarray(inputs["L0_ffn_w12"], f).transpose(0, 2, 1)
    g["w12P"] = np.ascontiguousarray(
        np.stack([panels(w12T[e], 128) for e in range(8)]))
    w3T = np.asarray(inputs["L0_ffn_w3"], f).transpose(0, 2, 1)
    g["w3P"] = np.ascontiguousarray(
        np.stack([panels(w3T[e], 128) for e in range(8)]))
    g["qb"] = np.asarray(inputs["L0_attn_q_b"], f).reshape(1, D)
    g["kb"] = np.asarray(inputs["L0_attn_k_b"], f).reshape(1, D)
    g["vb"] = np.asarray(inputs["L0_attn_v_b"], f).reshape(1, D)
    g["ob"] = np.asarray(inputs["L0_attn_o_b"], f).reshape(1, D)
    g["ebias"] = np.asarray(inputs["L0_ffn_expert_bias"], f).reshape(1, NEXP)
    g["eye_in"] = np.eye(128, dtype=f)
    g["ones_f32_in"] = np.ones((1, 128), f)
    g["ones_bfr_in"] = np.ones((1, 512), ml_dtypes.bfloat16)
    g["ones_bfc_in"] = np.ones((128, 1), ml_dtypes.bfloat16)
    sel = np.zeros((8, 8 * 128), ml_dtypes.bfloat16)
    for e in range(8):
        sel[e, 128 * e:128 * (e + 1)] = 1
    g["sel_in"] = sel
    in_maps = []
    for c in range(8):
        bb, hf = c // 2, c % 2
        m = dict(g)
        m["xT_mine"] = np.ascontiguousarray(
            xt[bb, :, TOK * hf:TOK * (hf + 1)])
        in_maps.append(m)
    return in_maps


def kernel(**inputs):
    nc = _get_nc(STAGES, DEBUG_OUT)
    in_maps = _prep_host(inputs)
    res = run_bass_kernel_spmd(nc, in_maps, core_ids=list(range(8)))
    out = np.empty((4, 2048, D), np.float32)
    for c in range(8):
        bb, hf = c // 2, c % 2
        out[bb, TOK * hf:TOK * (hf + 1), :] = res.results[c]["outT"].T
    return out
